# version 01
# baseline (speedup 1.0000x reference)
import numpy as np
from contextlib import ExitStack

import concourse.bass as bass
import concourse.mybir as mybir
from concourse.bass_utils import run_bass_kernel_spmd

F32 = mybir.dt.float32
BF16 = mybir.dt.bfloat16
I32 = mybir.dt.int32
I16 = mybir.dt.int16
F16 = mybir.dt.float16
AF = mybir.ActivationFunctionType
ALU = mybir.AluOpType
AX = mybir.AxisListType.X

TOK = 2048
D = 2048
KT = 16
H = 16
NCH = 16
W_T = 2064
I_MLP = 8192
I_F = 5632
NEWTON_ITERS = 7
QKV_CHUNKS = [(13, 512), (525, 512), (1037, 512), (1549, 512), (2061, 3)]
ENG_SEM = {"tensor": "sT", "vector": "sV", "scalar": "sA", "gpsimd": "sG", "sync": "sS"}

PHASE_LIMIT = 99
DBG = ()


class KB:
    def __init__(self, nc, block, stack):
        self.nc = nc
        self.block = block
        self.stack = stack
        self.sem = {}
        self.cnt = {}
        self.ver = {"qS": 0, "qG": 0}
        for s in ["qS.0", "qG.0", "sT", "sV", "sA", "sG", "sS"]:
            self.sem[s] = stack.enter_context(nc.semaphore(s.replace(".", "_")))
            self.cnt[s] = 0

    def cur(self, q):
        return f"{q}.{self.ver[q]}"

    def new_epoch(self):
        for q in ("qS", "qG"):
            self.ver[q] += 1
            nm = self.cur(q)
            self.sem[nm] = self.stack.enter_context(self.nc.semaphore(nm.replace(".", "_")))
            self.cnt[nm] = 0

    def step(self, eng, body, waits=()):
        sname = ENG_SEM[eng]
        tok = {}

        def run(e):
            for wt in waits:
                if wt is None:
                    continue
                s, v = wt
                if v > 0:
                    e.wait_ge(self.sem[s], v)
            last = body(e)
            assert last is not None, "step body must return its last instruction"
            self.cnt[sname] += 1
            if eng == "scalar":
                e.drain().then_inc(self.sem[sname], 1)
            else:
                last.then_inc(self.sem[sname], 1)
            tok["v"] = (sname, self.cnt[sname])

        getattr(self.block, eng)(run)
        return tok["v"]

    def dma(self, eng, out, in_, waits=(), **kw):
        q = self.cur("qS" if eng == "sync" else "qG")
        tok = {}

        def run(e):
            for wt in waits:
                if wt is None:
                    continue
                s, v = wt
                if v > 0:
                    e.wait_ge(self.sem[s], v)
            self.cnt[q] += 16
            e.dma_start(out=out, in_=in_, **kw).then_inc(self.sem[q], 16)
            tok["v"] = (q, self.cnt[q])

        getattr(self.block, eng)(run)
        return tok["v"]

    def dma_tr(self, out, in_, waits=()):
        q = self.cur("qS")
        tok = {}

        def run(e):
            for wt in waits:
                if wt is None:
                    continue
                s, v = wt
                if v > 0:
                    e.wait_ge(self.sem[s], v)
            self.cnt[q] += 16
            e.dma_start_transpose(out, in_).then_inc(self.sem[q], 16)
            tok["v"] = (q, self.cnt[q])

        self.block.sync(run)
        return tok["v"]

    def t(self, s):
        if s in ("qS", "qG"):
            s = self.cur(s)
        return (s, self.cnt[s])

    def barrier(self):
        toks = [(s, v) for s, v in self.cnt.items() if v > 0 and s != "sS"]
        return self.step("sync", lambda e: e.nop(), waits=toks)

    def edge(self, e, ins, eng="scalar"):
        s = ENG_SEM[eng]
        self.cnt[s] += 1
        ins.then_inc(self.sem[s], 1)
        e.wait_ge(self.sem[s], self.cnt[s])


def host_constants():
    import ml_dtypes
    ident = np.eye(128, dtype=np.float32)
    utri = (np.arange(128)[:, None] <= np.arange(128)[None, :]).astype(np.float32)
    strictu = (np.arange(128)[:, None] < np.arange(128)[None, :]).astype(np.float32)
    upneg = np.where(np.arange(128)[:, None] <= np.arange(128)[None, :], 0.0, -1e9).astype(np.float32)
    hsel = np.zeros((16, 2048), np.float32)
    for i in range(16):
        hsel[i, i * 128:(i + 1) * 128] = 1.0
    b = lambda a: a.astype(ml_dtypes.bfloat16)
    return {"identf": ident, "identb": b(ident), "i2": b(2 * ident), "utri": utri,
            "strictu": strictu, "upneg": upneg, "hselb": b(hsel)}


def build_nc(dbg=(), phase_limit=99, p1_parts=("tiles", "spill", "r2b", "hist"),
             p9_parts=("deq", "gu", "down"), skip_mid=False):
    nc = bass.Bass("TRN2", target_bir_lowering=False)

    inp = {}
    def EIN(name, shape, dt):
        inp[name] = nc.dram_tensor(name, shape, dt, kind="ExternalInput").ap()
    EIN("x", [TOK, D], F32)
    EIN("xh", [3, D], F32)
    EIN("wq", [D, D], F32); EIN("wk", [D, D], F32); EIN("wv", [D, D], F32)
    EIN("convw", [128, 3 * KT * 4], F32)
    EIN("wfa", [D, 128], F32); EIN("wfb", [128, H], F32); EIN("wb", [D, H], F32)
    EIN("wga", [D, 128], F32); EIN("wgb", [128, D], F32)
    EIN("alog", [H, 1], F32); EIN("dtb", [H, 1], F32)
    EIN("onorm", [128, 1], F32)
    EIN("wo", [D, D], F32)
    EIN("ln1w", [128, KT], F32)
    EIN("ln2w", [1, D], F32)
    EIN("wgate", [D, I_MLP], F32); EIN("wup", [D, I_MLP], F32); EIN("wdown", [I_MLP, D], F32)
    EIN("gup", [2 * I_F, D // 8], I32); EIN("gus", [D // 128, 2 * I_F], F16)
    EIN("dnp", [D, I_F // 8], I32); EIN("dns", [I_F // 128, D], F16)
    for nm, sh, dt in [("identf", [128, 128], F32), ("identb", [128, 128], BF16),
                      ("i2", [128, 128], BF16), ("utri", [128, 128], F32),
                      ("strictu", [128, 128], F32), ("upneg", [128, 128], F32),
                      ("hselb", [16, 2048], BF16)]:
        EIN(nm, sh, dt)

    out = nc.dram_tensor("out", [TOK, D], F32, kind="ExternalOutput").ap()

    def SCR(name, shape, dt):
        kind = "ExternalOutput" if name in dbg else "Internal"
        return nc.dram_tensor(name, shape, dt, kind=kind).ap()
    S_XN = SCR("S_XN", [TOK, D], BF16)
    S_QT = SCR("S_QT", [D, TOK], BF16)
    S_KT = SCR("S_KT", [D, TOK], BF16)
    S_VT = SCR("S_VT", [D, TOK], BF16)
    S_OT = SCR("S_OT", [TOK, D], BF16)
    S_OG = SCR("S_OG", [TOK, D], BF16)
    S_GATE = SCR("S_GATE", [TOK, D], BF16)
    S_X1 = SCR("S_X1", [TOK, D], F32)
    S_H2 = SCR("S_H2", [TOK, D], BF16)
    S_S = SCR("S_S", [TOK, D], F32)
    S_M = SCR("S_M", [I_MLP, TOK], BF16)
    S_WDNB = SCR("S_WDNB", [I_MLP, D], BF16)
    S_WGU = SCR("S_WGU", [2 * I_F, D], BF16)
    S_WDN4 = SCR("S_WDN4", [D, I_F], BF16)
    S_H4 = SCR("S_H4", [I_F, TOK], BF16)
    S_DBG = SCR("S_DBG", [128, 64], F32)
    S_DBG2 = SCR("S_DBG2", [128, D], BF16)
    S_DBG3 = SCR("S_DBG3", [128, 3 * 1024], F32)
    S_GB = SCR("S_GB", [H, TOK], F32)
    S_BB = SCR("S_BB", [H, TOK], F32)

    ST = ExitStack()
    with ST:
        E = ST.enter_context
        identf = E(nc.sbuf_tensor("sb_identf", [128, 128], F32))
        identb = E(nc.sbuf_tensor("sb_identb", [128, 128], BF16))
        i2c = E(nc.sbuf_tensor("sb_i2c", [128, 128], BF16))
        utri = E(nc.sbuf_tensor("sb_utri", [128, 128], F32))
        strictu = E(nc.sbuf_tensor("sb_strictu", [128, 128], F32))
        upneg = E(nc.sbuf_tensor("sb_upneg", [128, 128], F32))
        hselb = E(nc.sbuf_tensor("sb_hselb", [16, 2048], BF16))
        onesf_r = E(nc.sbuf_tensor("onesf_r", [1, 128], F32))
        onesf_c = E(nc.sbuf_tensor("onesf_c", [128, 1], F32))
        onesb128 = E(nc.sbuf_tensor("onesb128", [128, 128], BF16))
        ln1w = E(nc.sbuf_tensor("sb_ln1w", [128, KT], F32))
        ln2bc = E(nc.sbuf_tensor("ln2bc", [128, D], BF16))
        convw = E(nc.sbuf_tensor("sb_convw", [128, 3 * KT * 4], F32))
        wfb_sb = E(nc.sbuf_tensor("wfb_sb", [128, H], BF16))
        alog = E(nc.sbuf_tensor("sb_alog", [H, 1], F32))
        dtb = E(nc.sbuf_tensor("sb_dtb", [H, 1], F32))
        nega = E(nc.sbuf_tensor("nega", [H, 1], F32))
        onormc = E(nc.sbuf_tensor("onormc", [128, 1], F32))
        g_all = E(nc.sbuf_tensor("g_all", [H, TOK], F32))
        b_all = E(nc.sbuf_tensor("b_all", [H, TOK], F32))
        r2b = E(nc.sbuf_tensor("r2b", [128, TOK], BF16))
        r2cols = E(nc.sbuf_tensor("r2cols", [128, KT], BF16))
        hh = E(nc.sbuf_tensor("hh", [3, D], F32))
        gPos = E(nc.sbuf_tensor("gPos", [128, 256], F32))
        bPos = E(nc.sbuf_tensor("bPos", [128, 256], F32))
        bPosb = E(nc.sbuf_tensor("bPosb", [128, 256], BF16))
        Gpos = E(nc.sbuf_tensor("Gpos", [128, 256], F32))
        GCb = E(nc.sbuf_tensor("GCb", [128, 256], F32))
        dGC = E(nc.sbuf_tensor("dGC", [128, 256], F32))
        lamb = E(nc.sbuf_tensor("lamb", [128, 256], BF16))
        neglamb = E(nc.sbuf_tensor("neglamb", [128, 256], BF16))
        lamCdb = E(nc.sbuf_tensor("lamCdb", [128, 256], BF16))
        lamC = E(nc.sbuf_tensor("lamC", [128, 256], F32))
        lamf = E(nc.sbuf_tensor("lamf", [128, 256], F32))

        K = KB(nc, E(nc.Block()), ST)
        sem, cnt = K.sem, K.cnt

        def W3(t):
            return t.ap().rearrange("p (h n) -> p h n", h=H)

        def CB(src, c, n=128):
            return src[:, c * H:(c + 1) * H].unsqueeze(2).broadcast_to([128, H, n])

        def HB(t, n=H):
            return t.ap().unsqueeze(1).broadcast_to([128, n, 128])

        def p0_loads(e):
            for nm, dst in [("identf", identf), ("identb", identb), ("i2", i2c),
                            ("utri", utri), ("strictu", strictu), ("upneg", upneg),
                            ("hselb", hselb), ("ln1w", ln1w), ("convw", convw),
                            ("alog", alog), ("dtb", dtb), ("onorm", onormc)]:
                cnt["qS.0"] += 16
                e.dma_start(out=dst[:], in_=inp[nm]).then_inc(sem["qS.0"], 16)
        K.block.sync(p0_loads)
        t_p0s = K.t("qS")
        t_wfb = K.dma("gpsimd", wfb_sb[:], inp["wfb"])

        def p0_v(e):
            e.memset(onesf_r[:], 1.0)
            e.memset(onesf_c[:], 1.0)
            return e.memset(onesb128[:], 1.0)
        t_ones = K.step("vector", p0_v, waits=[t_p0s])
        t_nega0 = K.step("scalar", lambda e: e.activation(nega[:], alog[:], AF.Exp), waits=[t_p0s])
        t_nega = K.step("vector", lambda e: e.tensor_scalar(nega[:], nega[:], -1.0, None, ALU.mult),
                        waits=[t_nega0])

        ph0 = ExitStack()
        ln2row = ph0.enter_context(nc.sbuf_tensor("ln2row", [1, D], F32))
        ps_a = ph0.enter_context(nc.psum_tensor("ps_a", [128, 512], F32))
        prev = K.dma("sync", ln2row[:], inp["ln2w"])
        for ch in range(4):
            t_mm = K.step("tensor", lambda e, ch=ch: e.matmul(
                ps_a[:, 0:512], onesf_r[:], ln2row[:, ch * 512:(ch + 1) * 512],
                start=True, stop=True), waits=[t_ones, prev])
            prev = K.step("vector", lambda e, ch=ch: e.tensor_copy(
                ln2bc[:, ch * 512:(ch + 1) * 512], ps_a[:, 0:512]), waits=[t_mm])
        t_ln2bc = prev
        ph0.close()

        t_r2b = t_hh = None
        if phase_limit >= 1:
            ph1 = ExitStack()
            e1 = ph1.enter_context
            xt = [e1(nc.sbuf_tensor(f"p1x{i}", [128, D], F32)) for i in range(2)]
            xnb = [e1(nc.sbuf_tensor(f"p1xn{i}", [128, D], BF16)) for i in range(2)]
            sq = e1(nc.sbuf_tensor("p1sq", [128, D], F32))
            st_ = e1(nc.sbuf_tensor("p1st", [128, 12], F32))
            r2t = e1(nc.sbuf_tensor("p1r2t", [16, 128], BF16))
            xh_sb = e1(nc.sbuf_tensor("p1xh", [3, D], F32))
            hs_ = e1(nc.sbuf_tensor("p1hst", [3, 12], F32))
            psb_t = e1(nc.psum_tensor("p1pst", [16, 1024], BF16))
            ps_bb = e1(nc.psum_tensor("p1bb", [128, 512], F32))

            BAR = K.barrier()
            v_fin = [None] * KT
            sp_fin = [None] * KT
            for t in range(KT):
                p = t % 2
                w = [v_fin[t - 2], sp_fin[t - 2]] if t >= 2 else [BAR]
                t_ld = K.dma("sync", xt[p][:], inp["x"][t * 128:(t + 1) * 128, :], waits=w)
                t_sq = K.step("scalar", lambda e, p=p: e.activation(
                    sq[:], xt[p][:], AF.Square, accum_out=st_[:, 2:3]), waits=[t_ld])
                def v1(e, p=p):
                    K.edge(e, e.tensor_reduce(st_[:, 0:1], xt[p][:], AX, ALU.add), "vector")
                    K.edge(e, e.tensor_scalar(st_[:, 1:2], st_[:, 0:1], 1.0 / D, None, ALU.mult), "vector")
                    K.edge(e, e.tensor_scalar(st_[:, 3:4], st_[:, 2:3], 1.0 / D, None, ALU.mult), "vector")
                    K.edge(e, e.tensor_tensor(st_[:, 4:5], st_[:, 1:2], st_[:, 1:2], ALU.mult), "vector")
                    K.edge(e, e.tensor_sub(st_[:, 5:6], st_[:, 3:4], st_[:, 4:5]), "vector")
                    return e.tensor_scalar(st_[:, 6:7], st_[:, 5:6], 1e-5, None, ALU.add)
                t_v1 = K.step("vector", v1, waits=[t_ld, t_sq])
                def a2(e):
                    K.edge(e, e.activation(st_[:, 7:8], st_[:, 6:7], AF.Ln))
                    return e.activation(st_[:, 8:9], st_[:, 7:8], AF.Exp, scale=-0.5)
                t_a2 = K.step("scalar", a2, waits=[t_v1])
                def v2(e, p=p):
                    e.tensor_scalar(xnb[p][:], xt[p][:], st_[:, 1:2], st_[:, 8:9],
                                    ALU.subtract, ALU.mult)
                    K.edge(e, e.reciprocal(st_[:, 9:10], st_[:, 6:7]), "vector")
                    K.edge(e, e.tensor_tensor(st_[:, 10:11], st_[:, 5:6], st_[:, 9:10], ALU.mult), "vector")
                    return e.tensor_scalar(st_[:, 11:12], st_[:, 10:11], 1e-6, None, ALU.add)
                t_v2 = K.step("vector", v2, waits=[t_a2])
                def a3(e):
                    K.edge(e, e.activation(st_[:, 7:8], st_[:, 11:12], AF.Ln))
                    return e.activation(st_[:, 0:1], st_[:, 7:8], AF.Exp, scale=-0.5)
                t_a3 = K.step("scalar", a3, waits=[t_v2])
                v_fin[t] = K.step("vector", lambda e, t=t: e.tensor_copy(
                    r2cols[:, t:t + 1], st_[:, 0:1]), waits=[t_a3])
                if "spill" in p1_parts:
                    sp_fin[t] = K.dma("sync", S_XN[t * 128:(t + 1) * 128, :], xnb[p][:], waits=[t_v2])
                else:
                    sp_fin[t] = None
            if "r2b" not in p1_parts:
                t_r2b = v_fin[KT - 1]
                t_hh = v_fin[KT - 1]
            t_tr = K.step("tensor", lambda e: e.transpose(psb_t[:, 0:128], r2cols[:], identb[:]),
                          waits=[v_fin[KT - 1]]) if "r2b" in p1_parts else None
            if "r2b" in p1_parts:
                t_r2t = K.step("vector", lambda e: e.tensor_copy(r2t[:], psb_t[:, 0:128]), waits=[t_tr])
                prev = t_r2t
                for i in range(KT):
                    t_mm = K.step("tensor", lambda e, i=i: e.matmul(
                        ps_bb[:, 0:128], hselb[:, i * 128:(i + 1) * 128], r2t[:], start=True, stop=True),
                        waits=[prev])
                    prev = K.step("scalar", lambda e, i=i: e.copy(
                        r2b[:, i * 128:(i + 1) * 128], ps_bb[:, 0:128]), waits=[t_mm])
                t_r2b = prev
            else:
                t_r2b = v_fin[KT - 1]
            if "hist" not in p1_parts:
                t_hh = t_r2b
            t_xh = K.dma("sync", xh_sb[:], inp["xh"]) if "hist" in p1_parts else None
            if "hist" in p1_parts:
                t_hsq = K.step("scalar", lambda e: e.activation(
                    sq[0:3, :], xh_sb[:], AF.Square, accum_out=hs_[:, 2:3]), waits=[t_xh])
                def hv1(e):
                    K.edge(e, e.tensor_reduce(hs_[:, 0:1], xh_sb[:], AX, ALU.add), "vector")
                    K.edge(e, e.tensor_scalar(hs_[:, 1:2], hs_[:, 0:1], 1.0 / D, None, ALU.mult), "vector")
                    K.edge(e, e.tensor_scalar(hs_[:, 3:4], hs_[:, 2:3], 1.0 / D, None, ALU.mult), "vector")
                    K.edge(e, e.tensor_tensor(hs_[:, 4:5], hs_[:, 1:2], hs_[:, 1:2], ALU.mult), "vector")
                    K.edge(e, e.tensor_sub(hs_[:, 5:6], hs_[:, 3:4], hs_[:, 4:5]), "vector")
                    return e.tensor_scalar(hs_[:, 6:7], hs_[:, 5:6], 1e-5, None, ALU.add)
                t_hv1 = K.step("vector", hv1, waits=[t_xh, t_hsq])
                def ha1(e):
                    K.edge(e, e.activation(hs_[:, 7:8], hs_[:, 6:7], AF.Ln))
                    return e.activation(hs_[:, 8:9], hs_[:, 7:8], AF.Exp, scale=-0.5)
                t_ha1 = K.step("scalar", ha1, waits=[t_hv1])
                def hv2(e):
                    K.edge(e, e.reciprocal(hs_[:, 9:10], hs_[:, 6:7]), "vector")
                    K.edge(e, e.tensor_tensor(hs_[:, 10:11], hs_[:, 5:6], hs_[:, 9:10], ALU.mult), "vector")
                    return e.tensor_scalar(hs_[:, 11:12], hs_[:, 10:11], 1e-6, None, ALU.add)
                t_hv2 = K.step("vector", hv2, waits=[t_ha1])
                def ha2(e):
                    K.edge(e, e.activation(hs_[:, 7:8], hs_[:, 11:12], AF.Ln))
                    return e.activation(hs_[:, 0:1], hs_[:, 7:8], AF.Exp, scale=-0.5)
                t_ha2 = K.step("scalar", ha2, waits=[t_hv2])
                def hv3(e):
                    K.edge(e, e.tensor_tensor(hs_[:, 9:10], hs_[:, 8:9], hs_[:, 0:1], ALU.mult), "vector")
                    return e.tensor_scalar(hh[:], xh_sb[:], hs_[:, 1:2], hs_[:, 9:10],
                                           ALU.subtract, ALU.mult)
                t_hh = K.step("vector", hv3, waits=[t_ha2])
            ph1.close()

        t_hT = t_hThist = None
        t_p4_end = None
        if phase_limit >= 2 and not skip_mid:
            ph2 = ExitStack()
            e2 = ph2.enter_context
            hT = e2(nc.sbuf_tensor("hT", [128, KT * W_T], BF16))
            xnt_t = [e2(nc.sbuf_tensor(f"p2xt{i}", [128, D], BF16)) for i in range(2)]
            ps_h3 = e2(nc.psum_tensor("p2ph", [128, 512], F32))

            BAR2 = K.barrier()
            prev_tr = None
            for dt_i in range(KT):
                t_tr = K.step("tensor", lambda e, dt_i=dt_i: e.transpose(
                    ps_h3[:, 0:3], hh[:, dt_i * 128:(dt_i + 1) * 128], identf[0:3, 0:3]),
                    waits=[t_hh, prev_tr, BAR2])
                prev_tr = K.step("scalar", lambda e, dt_i=dt_i: e.mul(
                    hT[:, dt_i * W_T + 13:dt_i * W_T + 16], ps_h3[:, 0:3], ln1w[:, dt_i:dt_i + 1]),
                    waits=[t_tr])
            t_hThist = prev_tr
            ht_done = [None] * KT
            for dt_i in range(KT):
                p = dt_i % 2
                w = [t_r2b, BAR2] + ([ht_done[dt_i - 2]] if dt_i >= 2 else [K.t("qS")])
                t_tl = K.dma_tr(xnt_t[p][:], S_XN[:, dt_i * 128:(dt_i + 1) * 128], waits=w)
                ht_done[dt_i] = K.step("vector", lambda e, dt_i=dt_i, p=p: e.scalar_tensor_tensor(
                    hT[:, dt_i * W_T + 16:(dt_i + 1) * W_T], xnt_t[p][:], ln1w[:, dt_i:dt_i + 1],
                    r2b[:], ALU.mult, ALU.mult), waits=[t_tl])
            t_hT = ht_done[KT - 1]

            t_lamfam = None
            if phase_limit >= 3:
                ph3 = ExitStack()
                e3 = ph3.enter_context
                wfa_sb = e3(nc.sbuf_tensor("wfa_sb", [128, D], BF16))
                wga_sb = e3(nc.sbuf_tensor("wga_sb", [128, D], BF16))
                wb_sb = e3(nc.sbuf_tensor("wb_sb", [128, KT * H], BF16))
                wgb_sb = e3(nc.sbuf_tensor("wgb_sb", [128, D], BF16))
                faT = e3(nc.sbuf_tensor("faT", [128, TOK], BF16))
                gaT = e3(nc.sbuf_tensor("gaT", [128, TOK], BF16))
                gpre = e3(nc.sbuf_tensor("gpre", [H, TOK], F32))
                sps = e3(nc.sbuf_tensor("sps", [H, TOK], F32))
                sps2 = e3(nc.sbuf_tensor("sps2", [H, TOK], F32))
                ps_w = e3(nc.psum_tensor("p3ps", [128, 512], F32))
                ps_w2 = e3(nc.psum_tensor("p3ps2", [128, 512], F32))
                ps_g16 = e3(nc.psum_tensor("p3psg", [16, 512], F32))
                gt_sb = [e3(nc.sbuf_tensor(f"gt{i}", [128, 512], BF16)) for i in range(2)]

                BAR3 = K.barrier()
                t_wfa = K.dma("gpsimd", wfa_sb.ap().rearrange("p (kt m) -> p kt m", kt=KT),
                              inp["wfa"].rearrange("(kt p) m -> p kt m", p=128),
                              waits=[BAR3])
                t_wga = K.dma("gpsimd", wga_sb.ap().rearrange("p (kt m) -> p kt m", kt=KT),
                              inp["wga"].rearrange("(kt p) m -> p kt m", p=128))
                t_wb = K.dma("gpsimd", wb_sb.ap().rearrange("p (kt m) -> p kt m", kt=KT),
                             inp["wb"].rearrange("(kt p) m -> p kt m", p=128))
                t_wgb = K.dma("gpsimd", wgb_sb[:], inp["wgb"])

                t_faT = t_gaT = None
                for wsb, dst, tw in [(wfa_sb, faT, t_wfa), (wga_sb, gaT, t_wga)]:
                    prev_ev = None
                    for ch in range(4):
                        pp = ps_w if ch % 2 == 0 else ps_w2
                        def mm(e, wsb=wsb, ch=ch, pp=pp):
                            ins = None
                            for k in range(KT):
                                ins = e.matmul(pp[:, 0:512], wsb[:, k * 128:(k + 1) * 128],
                                               hT[:, k * W_T + 16 + ch * 512:k * W_T + 16 + (ch + 1) * 512],
                                               start=(k == 0), stop=(k == KT - 1))
                            return ins
                        t_mm = K.step("tensor", mm, waits=[t_hT, tw, prev_ev])
                        prev_ev = K.step("scalar", lambda e, dst=dst, ch=ch, pp=pp: e.copy(
                            dst[:, ch * 512:(ch + 1) * 512], pp[:, 0:512]), waits=[t_mm])
                    if wsb is wfa_sb:
                        t_faT = prev_ev
                    else:
                        t_gaT = prev_ev
                prev_ev = None
                for ch in range(4):
                    t_mm = K.step("tensor", lambda e, ch=ch: e.matmul(
                        ps_g16[:, 0:512], wfb_sb[:], faT[:, ch * 512:(ch + 1) * 512],
                        start=True, stop=True), waits=[t_faT, t_wfb, prev_ev])
                    prev_ev = K.step("vector", lambda e, ch=ch: e.tensor_scalar(
                        gpre[:, ch * 512:(ch + 1) * 512], ps_g16[:, 0:512], dtb[:, 0:1], None,
                        ALU.add), waits=[t_mm])
                t_gpre = prev_ev
                def asp(e):
                    K.edge(e, e.activation(sps[:], gpre[:], AF.Abs))
                    K.edge(e, e.activation(sps2[:], sps[:], AF.Exp, scale=-1.0))
                    return e.activation(sps[:], sps2[:], AF.Ln, bias=1.0)
                t_sp2 = K.step("scalar", asp, waits=[t_gpre])
                def vsp2(e):
                    e.tensor_single_scalar(sps2[:], gpre[:], 0.0, ALU.max)
                    e.tensor_add(gpre[:], sps2[:], sps[:])
                    return e.tensor_scalar(g_all[:], gpre[:], nega[:, 0:1], None, ALU.mult)
                t_g = K.step("vector", vsp2, waits=[t_sp2, t_nega])
                prev_ev = t_g
                for ch in range(4):
                    def mmb(e, ch=ch):
                        ins = None
                        for k in range(KT):
                            ins = e.matmul(ps_g16[:, 0:512], wb_sb[:, k * H:(k + 1) * H],
                                           hT[:, k * W_T + 16 + ch * 512:k * W_T + 16 + (ch + 1) * 512],
                                           start=(k == 0), stop=(k == KT - 1))
                        return ins
                    t_mm = K.step("tensor", mmb, waits=[t_hT, t_wb, prev_ev])
                    prev_ev = K.step("scalar", lambda e, ch=ch: e.activation(
                        b_all[:, ch * 512:(ch + 1) * 512], ps_g16[:, 0:512], AF.Sigmoid),
                        waits=[t_mm])
                t_b = prev_ev
                dma_gate = [None, None]
                for pt in range(KT):
                    for ch in range(4):
                        i = pt * 4 + ch
                        pp = ps_w if i % 2 == 0 else ps_w2
                        gb = gt_sb[i % 2]
                        t_mm = K.step("tensor", lambda e, pt=pt, ch=ch, pp=pp: e.matmul(
                            pp[:, 0:512], gaT[:, pt * 128:(pt + 1) * 128],
                            wgb_sb[:, ch * 512:(ch + 1) * 512], start=True, stop=True),
                            waits=[t_gaT, t_wgb, dma_gate[i % 2]])
                        t_sil = K.step("scalar", lambda e, pp=pp, gb=gb: e.activation(
                            gb[:], pp[:, 0:512], AF.Silu), waits=[t_mm])
                        dma_gate[i % 2] = K.dma(
                            "sync", S_GATE[pt * 128:(pt + 1) * 128, ch * 512:(ch + 1) * 512],
                            gb[:], waits=[t_sil])
                ps_t16 = e3(nc.psum_tensor("p3pt16", [128, 512], F32))
                prev_ev = None
                for src, dst, twt in [(g_all, gPos, t_g), (b_all, bPos, t_b)]:
                    for c in range(NCH):
                        t_tr = K.step("tensor", lambda e, src=src, c=c: e.transpose(
                            ps_t16[:, 0:16], src[0:16, c * 128:(c + 1) * 128], identf[0:16, 0:16]),
                            waits=[twt, prev_ev])
                        prev_ev = K.step("vector", lambda e, dst=dst, c=c: e.tensor_copy(
                            dst[:, c * 16:(c + 1) * 16], ps_t16[:, 0:16]), waits=[t_tr])
                t_pos = prev_ev
                ps_256 = e3(nc.psum_tensor("p3p256", [128, 512], F32))
                ps_1r = e3(nc.psum_tensor("p3p1r", [1, 512], F32))
                gcrow = e3(nc.sbuf_tensor("gcrow", [1, 256], F32))
                t_cs = K.step("tensor", lambda e: e.matmul(
                    ps_256[:, 0:256], utri[:], gPos[:], start=True, stop=True), waits=[t_pos])
                t_G = K.step("vector", lambda e: e.tensor_copy(Gpos[:], ps_256[:, 0:256]), waits=[t_cs])
                t_gc1 = K.step("tensor", lambda e: e.matmul(
                    ps_1r[:, 0:256], onesf_c[:], gPos[:], start=True, stop=True), waits=[t_pos, t_G])
                t_gc2 = K.step("vector", lambda e: e.tensor_copy(gcrow[:], ps_1r[:, 0:256]), waits=[t_gc1])
                t_gc3 = K.step("tensor", lambda e: e.matmul(
                    ps_256[:, 0:256], onesf_r[:], gcrow[:], start=True, stop=True), waits=[t_gc2])
                t_GCb = K.step("vector", lambda e: e.tensor_copy(GCb[:], ps_256[:, 0:256]), waits=[t_gc3])
                t_lam = K.step("scalar", lambda e: e.activation(lamf[:], Gpos[:], AF.Exp),
                               waits=[t_G])
                def vlam(e):
                    e.tensor_copy(lamb[:], lamf[:])
                    e.tensor_scalar(neglamb[:], lamf[:], -1.0, None, ALU.mult)
                    e.tensor_copy(bPosb[:], bPos[:])
                    return e.tensor_sub(dGC[:], GCb[:], Gpos[:])
                t_lamv = K.step("vector", vlam, waits=[t_lam, t_GCb, t_pos])
                def alam(e):
                    e.activation(lamC[:], GCb[:], AF.Exp)
                    return e.activation(lamCdb[:], dGC[:], AF.Exp)
                t_lamfam = K.step("scalar", alam, waits=[t_lamv])
                if "S_GB" in dbg:
                    K.dma("sync", S_GB[:], g_all[:], waits=[t_g])
                if "S_BB" in dbg:
                    K.dma("sync", S_BB[:], b_all[:], waits=[t_b])
                ph3.close()

            if phase_limit >= 4:
                ph4 = ExitStack()
                e4 = ph4.enter_context
                wbuf = [e4(nc.sbuf_tensor(f"p4w{i}", [128, D], BF16)) for i in range(2)]
                qpre = [e4(nc.sbuf_tensor(f"p4qp{i}", [128, W_T], BF16)) for i in range(2)]
                cbuf1 = e4(nc.sbuf_tensor("p4c1", [128, TOK], BF16))
                cbuf2 = e4(nc.sbuf_tensor("p4c2", [128, TOK], BF16))
                silu_b = [e4(nc.sbuf_tensor(f"p4s{i}", [128, TOK], BF16)) for i in range(2)]
                sq_b = e4(nc.sbuf_tensor("p4sq", [128, TOK], BF16))
                rsf = e4(nc.sbuf_tensor("p4rs", [128, TOK], F32))
                rsb = e4(nc.sbuf_tensor("p4rsb", [128, TOK], BF16))
                outq = [e4(nc.sbuf_tensor(f"p4o{i}", [128, TOK], BF16)) for i in range(2)]
                psq = [e4(nc.psum_tensor(f"p4p{i}", [128, 512], F32)) for i in range(2)]
                psl = e4(nc.psum_tensor("p4pl", [128, 512], F32))

                BAR4 = K.barrier()
                mats = [("wq", S_QT, True, 0), ("wk", S_KT, True, 1), ("wv", S_VT, False, 2)]
                ev_done = {"__bar__": BAR4}
                conv_done = {}
                fin_done = {}
                rs_done = {}
                out_dma = [None, None]
                for mat_i, (wnm, sdst, do_l2, mslot) in enumerate(mats):
                    for dt_i in range(KT):
                        gi = mat_i * KT + dt_i
                        p = gi % 2
                        t_wld = None
                        for qq in range(2):
                            t_wld = K.dma(
                                "gpsimd",
                                wbuf[p][:, qq * 1024:(qq + 1) * 1024]
                                .rearrange("p (kt m) -> p kt m", kt=8),
                                inp[wnm][qq * 1024:(qq + 1) * 1024, dt_i * 128:(dt_i + 1) * 128]
                                .rearrange("(kt p) m -> p kt m", p=128),
                                waits=[ev_done.get(gi - 2), BAR4] if qq == 0 else ())
                        ev_toks = []
                        for ci, (c0, wd) in enumerate(QKV_CHUNKS):
                            pp = psq[ci % 2]
                            def mmq(e, dt_i=dt_i, c0=c0, wd=wd, pp=pp, p=p):
                                ins = None
                                for k in range(KT):
                                    ins = e.matmul(
                                        pp[:, 0:wd], wbuf[p][:, k * 128:(k + 1) * 128],
                                        hT[:, k * W_T + c0:k * W_T + c0 + wd],
                                        start=(k == 0), stop=(k == KT - 1))
                                return ins
                            wl = [t_hT, t_hThist, t_wld]
                            if ci >= 2:
                                wl.append(ev_toks[ci - 2])
                            elif gi >= 1:
                                wl.append(ev_done.get(gi - 1))
                            t_mm = K.step("tensor", mmq, waits=wl)
                            t_ev = K.step("scalar", lambda e, c0=c0, wd=wd, pp=pp, p=p: e.copy(
                                qpre[p][:, c0:c0 + wd], pp[:, 0:wd]),
                                waits=[t_mm, conv_done.get(gi - 2)])
                            ev_toks.append(t_ev)
                        ev_done[gi] = ev_toks[-1]
                        cw0 = mslot * 64 + dt_i * 4
                        def vconv(e, p=p, cw0=cw0):
                            e.tensor_scalar(cbuf1[:], qpre[p][:, 13:13 + TOK],
                                            convw[:, cw0:cw0 + 1], None, ALU.mult)
                            e.scalar_tensor_tensor(cbuf2[:], qpre[p][:, 14:14 + TOK],
                                                   convw[:, cw0 + 1:cw0 + 2], cbuf1[:],
                                                   ALU.mult, ALU.add)
                            e.scalar_tensor_tensor(cbuf1[:], qpre[p][:, 15:15 + TOK],
                                                   convw[:, cw0 + 2:cw0 + 3], cbuf2[:],
                                                   ALU.mult, ALU.add)
                            return e.scalar_tensor_tensor(cbuf2[:], qpre[p][:, 16:16 + TOK],
                                                          convw[:, cw0 + 3:cw0 + 4], cbuf1[:],
                                                          ALU.mult, ALU.add)
                        t_conv = K.step("vector", vconv, waits=[ev_done[gi], fin_done.get(gi - 1)])
                        conv_done[gi] = t_conv
                        t_silu = K.step("scalar", lambda e, p=p: e.activation(
                            silu_b[p][:], cbuf2[:], AF.Silu), waits=[t_conv, fin_done.get(gi - 2)])
                        if do_l2:
                            t_sq2 = K.step("scalar", lambda e, p=p: e.activation(
                                sq_b[:], silu_b[p][:], AF.Square),
                                waits=[t_silu, rs_done.get(gi - 1)])
                            sum_tok = None
                            for chh in range(4):
                                def mml(e, chh=chh):
                                    return e.matmul(psl[:, 0:512], onesb128[:],
                                                    sq_b[:, chh * 512:(chh + 1) * 512],
                                                    start=True, stop=True)
                                t_ml = K.step("tensor", mml,
                                              waits=[t_sq2, sum_tok, rs_done.get(gi - 1) if chh == 0 else None])
                                sum_tok = K.step("vector", lambda e, chh=chh: e.tensor_scalar(
                                    rsf[:, chh * 512:(chh + 1) * 512], psl[:, 0:512], 1e-6,
                                    None, ALU.add), waits=[t_ml])
                            def arsq(e):
                                K.edge(e, e.activation(rsf[:], rsf[:], AF.Ln))
                                return e.activation(rsb[:], rsf[:], AF.Exp, scale=-0.5)
                            t_rs = K.step("scalar", arsq,
                                          waits=[sum_tok, fin_done.get(gi - 1)])
                            rs_done[gi] = t_rs
                            scl = 0.08838834764831845 if wnm == "wq" else 1.0
                            t_fin = K.step("vector", lambda e, p=p, scl=scl: e.scalar_tensor_tensor(
                                outq[p][:], silu_b[p][:], scl, rsb[:], ALU.mult, ALU.mult),
                                waits=[t_rs, out_dma[p]])
                        else:
                            t_fin = K.step("vector", lambda e, p=p: e.tensor_copy(
                                outq[p][:], silu_b[p][:]), waits=[t_silu, out_dma[p]])
                        fin_done[gi] = t_fin
                        out_dma[p] = K.dma("sync", sdst[dt_i * 128:(dt_i + 1) * 128, :],
                                           outq[p][:], waits=[t_fin])
                ph4.close()
                t_p4_end = K.t("qS")
            ph2.close()

        if phase_limit >= 5 and not skip_mid:
            K.new_epoch()
            ph5 = ExitStack()
            e5 = ph5.enter_context
            qTc = [e5(nc.sbuf_tensor(f"p5q{i}", [128, TOK], BF16)) for i in range(2)]
            kTc = [e5(nc.sbuf_tensor(f"p5k{i}", [128, TOK], BF16)) for i in range(2)]
            kPc = [e5(nc.sbuf_tensor(f"p5kp{i}", [128, TOK], BF16)) for i in range(2)]
            vPc = [e5(nc.sbuf_tensor(f"p5vp{i}", [128, TOK], BF16)) for i in range(2)]
            w1w = e5(nc.sbuf_tensor("p5w1", [128, TOK], F32))
            gargw = e5(nc.sbuf_tensor("p5ga", [128, TOK], F32))
            dtuw = e5(nc.sbuf_tensor("p5dtu", [128, TOK], BF16))
            bstrw = e5(nc.sbuf_tensor("p5bst", [128, TOK], F32))
            a1w = e5(nc.sbuf_tensor("p5a1", [128, TOK], BF16))
            mctw = e5(nc.sbuf_tensor("p5mct", [128, TOK], BF16))
            aqtw = e5(nc.sbuf_tensor("p5aqt", [128, TOK], BF16))
            atw = e5(nc.sbuf_tensor("p5at", [128, TOK], BF16))
            xw = [e5(nc.sbuf_tensor(f"p5x{i}", [128, TOK], BF16)) for i in range(2)]
            xtw = [e5(nc.sbuf_tensor(f"p5xt{i}", [128, TOK], BF16)) for i in range(2)]
            qw = e5(nc.sbuf_tensor("p5qw", [128, TOK], BF16))
            trw = e5(nc.sbuf_tensor("p5tr", [128, TOK], BF16))
            rw = e5(nc.sbuf_tensor("p5rw", [128, TOK], BF16))
            uw = e5(nc.sbuf_tensor("p5uw", [128, TOK], BF16))
            tow = e5(nc.sbuf_tensor("p5to", [128, TOK], BF16))
            oww = e5(nc.sbuf_tensor("p5ow", [128, TOK], BF16))
            khw = e5(nc.sbuf_tensor("p5kh", [128, TOK], BF16))
            Sf = [e5(nc.sbuf_tensor(f"p5S{i}", [128, TOK], F32)) for i in range(2)]
            Sbf = [e5(nc.sbuf_tensor(f"p5Sb{i}", [128, TOK], BF16)) for i in range(2)]
            pf1 = e5(nc.psum_tensor("p5pf1", [128, TOK], F32))
            pf2 = e5(nc.psum_tensor("p5pf2", [128, TOK], F32))

            BAR5 = K.barrier()
            def vinit(e):
                e.memset(Sf[1][:], 0.0)
                return e.memset(Sbf[1][:], 0.0)
            t_prev_chunk = K.step("vector", vinit, waits=[t_lamfam, BAR5])
            t_ow_dma = None
            for c_i in range(NCH):
                p = c_i % 2
                sp = 1 - (c_i % 2)
                for qq in range(4):
                    K.dma("sync",
                          qTc[p][:, qq * 512:(qq + 1) * 512].rearrange("p (hh m) -> p hh m", hh=4),
                          S_QT[qq * 512:(qq + 1) * 512, c_i * 128:(c_i + 1) * 128]
                          .rearrange("(hh p) m -> p hh m", p=128),
                          waits=[t_p4_end, t_prev_chunk, BAR5] if qq == 0 else ())
                for qq in range(4):
                    K.dma("sync",
                          kTc[p][:, qq * 512:(qq + 1) * 512].rearrange("p (hh m) -> p hh m", hh=4),
                          S_KT[qq * 512:(qq + 1) * 512, c_i * 128:(c_i + 1) * 128]
                          .rearrange("(hh p) m -> p hh m", p=128))
                K.dma_tr(kPc[p][:], S_KT[:, c_i * 128:(c_i + 1) * 128])
                t_ld = K.dma_tr(vPc[p][:], S_VT[:, c_i * 128:(c_i + 1) * 128])
                def va1(e, c_i=c_i):
                    return e.tensor_tensor(W3(w1w), HB(upneg), CB(Gpos, c_i), ALU.subtract)
                t_w1 = K.step("vector", va1, waits=[t_prev_chunk])
                def ta2(e, c_i=c_i):
                    ins = None
                    for hx in range(H):
                        ins = e.transpose(pf1[:, hx * 128:(hx + 1) * 128],
                                          Gpos[:, c_i * H + hx:c_i * H + hx + 1]
                                          .broadcast_to([128, 128]), identf[:])
                    return ins
                t_gbc = K.step("tensor", ta2, waits=[t_w1])
                t_garg = K.step("vector", lambda e: e.tensor_add(gargw[:], pf1[:], w1w[:]),
                                waits=[t_gbc])
                t_dtu = K.step("scalar", lambda e: e.activation(dtuw[:], gargw[:], AF.Exp),
                               waits=[t_garg])
                def vb2(e, c_i=c_i):
                    return e.tensor_tensor(W3(bstrw), HB(strictu), CB(bPos, c_i), ALU.mult)
                t_bstr = K.step("vector", vb2, waits=[t_garg])
                def tkk(e, p=p):
                    ins = None
                    for hx in range(H):
                        ins = e.matmul(pf1[:, hx * 128:(hx + 1) * 128],
                                       kTc[p][:, hx * 128:(hx + 1) * 128],
                                       kTc[p][:, hx * 128:(hx + 1) * 128], start=True, stop=True)
                    return ins
                t_kk = K.step("tensor", tkk, waits=[t_ld, t_dtu])
                t_a1 = K.step("vector", lambda e: e.tensor_tensor(a1w[:], pf1[:], dtuw[:], ALU.mult),
                              waits=[t_kk, t_bstr])
                t_mct = K.step("vector", lambda e: e.tensor_tensor(mctw[:], a1w[:], bstrw[:], ALU.mult),
                               waits=[t_a1])
                def tkq(e, p=p):
                    ins = None
                    for hx in range(H):
                        ins = e.matmul(pf2[:, hx * 128:(hx + 1) * 128],
                                       kTc[p][:, hx * 128:(hx + 1) * 128],
                                       qTc[p][:, hx * 128:(hx + 1) * 128], start=True, stop=True)
                    return ins
                t_kq = K.step("tensor", tkq, waits=[t_ld, t_a1])
                t_aqt = K.step("vector", lambda e: e.tensor_tensor(aqtw[:], pf2[:], dtuw[:], ALU.mult),
                               waits=[t_kq])
                def vat(e):
                    e.tensor_tensor(W3(atw), W3(mctw), HB(identb), ALU.add)
                    e.tensor_copy(W3(xw[0]), HB(identb))
                    return e.tensor_copy(W3(xtw[0]), HB(identb))
                t_at = K.step("vector", vat, waits=[t_mct])
                cx = 0
                t_prevev = t_at
                for it in range(NEWTON_ITERS):
                    def tax(e, cx=cx):
                        ins = None
                        for hx in range(H):
                            ins = e.matmul(pf1[:, hx * 128:(hx + 1) * 128],
                                           atw[:, hx * 128:(hx + 1) * 128],
                                           xw[cx][:, hx * 128:(hx + 1) * 128],
                                           start=True, stop=True)
                        return ins
                    t_ax = K.step("tensor", tax, waits=[t_prevev, t_aqt])
                    t_q = K.step("vector", lambda e: e.tensor_tensor(
                        W3(qw), HB(i2c), pf1.ap().rearrange("p (h n) -> p h n", h=H),
                        ALU.subtract), waits=[t_ax])
                    def txn(e, cx=cx):
                        ins = None
                        for hx in range(H):
                            ins = e.matmul(pf1[:, hx * 128:(hx + 1) * 128],
                                           xtw[cx][:, hx * 128:(hx + 1) * 128],
                                           qw[:, hx * 128:(hx + 1) * 128], start=True, stop=True)
                        for hx in range(H):
                            ins = e.matmul(pf2[:, hx * 128:(hx + 1) * 128],
                                           qw[:, hx * 128:(hx + 1) * 128],
                                           xtw[cx][:, hx * 128:(hx + 1) * 128], start=True, stop=True)
                        return ins
                    t_xn = K.step("tensor", txn, waits=[t_q])
                    def aev(e, cx=cx):
                        e.copy(xw[1 - cx][:], pf1[:])
                        return e.copy(xtw[1 - cx][:], pf2[:])
                    t_prevev = K.step("scalar", aev, waits=[t_xn])
                    cx = 1 - cx
                t_newton = t_prevev
                def tks(e, p=p, sp=sp):
                    ins = None
                    for hx in range(H):
                        ins = e.matmul(pf1[:, hx * 128:(hx + 1) * 128],
                                       kTc[p][:, hx * 128:(hx + 1) * 128],
                                       Sbf[sp][:, hx * 128:(hx + 1) * 128], start=True, stop=True)
                    return ins
                t_ks = K.step("tensor", tks, waits=[t_newton])
                def vtr(e, c_i=c_i):
                    return e.tensor_tensor(W3(trw), pf1.ap().rearrange("p (h n) -> p h n", h=H),
                                           CB(neglamb, c_i), ALU.mult)
                t_tr2 = K.step("vector", vtr, waits=[t_ks])
                t_rw = K.step("vector", lambda e, p=p: e.tensor_add(rw[:], trw[:], vPc[p][:]),
                              waits=[t_tr2])
                def tw_(e, cx=cx):
                    ins = None
                    for hx in range(H):
                        ins = e.matmul(pf2[:, hx * 128:(hx + 1) * 128],
                                       xtw[cx][:, hx * 128:(hx + 1) * 128],
                                       rw[:, hx * 128:(hx + 1) * 128], start=True, stop=True)
                    return ins
                t_w = K.step("tensor", tw_, waits=[t_rw])
                def vu(e, c_i=c_i):
                    return e.tensor_tensor(W3(uw), pf2.ap().rearrange("p (h n) -> p h n", h=H),
                                           CB(bPosb, c_i), ALU.mult)
                t_u = K.step("vector", vu, waits=[t_w])
                def tqs(e, p=p, sp=sp):
                    ins = None
                    for hx in range(H):
                        ins = e.matmul(pf1[:, hx * 128:(hx + 1) * 128],
                                       qTc[p][:, hx * 128:(hx + 1) * 128],
                                       Sbf[sp][:, hx * 128:(hx + 1) * 128], start=True, stop=True)
                    return ins
                t_qs = K.step("tensor", tqs, waits=[t_tr2])
                def vto(e, c_i=c_i):
                    return e.tensor_tensor(W3(tow), pf1.ap().rearrange("p (h n) -> p h n", h=H),
                                           CB(lamb, c_i), ALU.mult)
                t_to = K.step("vector", vto, waits=[t_qs])
                def tau(e):
                    ins = None
                    for hx in range(H):
                        ins = e.matmul(pf2[:, hx * 128:(hx + 1) * 128],
                                       aqtw[:, hx * 128:(hx + 1) * 128],
                                       uw[:, hx * 128:(hx + 1) * 128], start=True, stop=True)
                    return ins
                t_au = K.step("tensor", tau, waits=[t_u, t_to])
                t_o = K.step("vector", lambda e: e.tensor_add(oww[:], tow[:], pf2[:]),
                             waits=[t_au, t_ow_dma])
                t_ow_dma = K.dma("sync", S_OT[c_i * 128:(c_i + 1) * 128, :], oww[:], waits=[t_o])
                def vkh(e, p=p, c_i=c_i):
                    return e.tensor_tensor(W3(khw), kPc[p].ap().rearrange("p (h n) -> p h n", h=H),
                                           CB(lamCdb, c_i), ALU.mult)
                t_kh = K.step("vector", vkh, waits=[t_o])
                def tku(e):
                    ins = None
                    for hx in range(H):
                        ins = e.matmul(pf1[:, hx * 128:(hx + 1) * 128],
                                       khw[:, hx * 128:(hx + 1) * 128],
                                       uw[:, hx * 128:(hx + 1) * 128], start=True, stop=True)
                    return ins
                t_ku = K.step("tensor", tku, waits=[t_kh, t_to])
                def vts(e, c_i=c_i, sp=sp):
                    return e.tensor_tensor(W3(w1w), Sf[sp].ap().rearrange("p (h n) -> p h n", h=H),
                                           CB(lamC, c_i), ALU.mult)
                t_ts = K.step("vector", vts, waits=[t_ku])
                t_Snew = K.step("vector", lambda e, sp=sp: e.tensor_add(Sf[1 - sp][:], w1w[:], pf1[:]),
                                waits=[t_ts])
                t_prev_chunk = K.step("scalar", lambda e, sp=sp: e.copy(Sbf[1 - sp][:], Sf[1 - sp][:]),
                                      waits=[t_Snew])
            t_p5_end = K.t("qS")
            ph5.close()

        if phase_limit >= 6 and not skip_mid:
            ph6 = ExitStack()
            e6 = ph6.enter_context
            otile = [e6(nc.sbuf_tensor(f"p6o{i}", [128, D], BF16)) for i in range(2)]
            gtile = [e6(nc.sbuf_tensor(f"p6g{i}", [128, D], BF16)) for i in range(2)]
            sqt = e6(nc.sbuf_tensor("p6sq", [128, D], BF16))
            mst = e6(nc.sbuf_tensor("p6ms", [128, H], F32))
            ms2 = e6(nc.sbuf_tensor("p6ms2", [128, H], F32))
            rst = e6(nc.sbuf_tensor("p6rs", [128, H], BF16))
            t1t = e6(nc.sbuf_tensor("p6t1", [128, D], BF16))
            ogt = [e6(nc.sbuf_tensor(f"p6og{i}", [128, D], BF16)) for i in range(2)]
            wdnb = [e6(nc.sbuf_tensor(f"p6wd{i}", [128, D], BF16)) for i in range(2)]
            BAR6 = K.barrier()
            wdn_dma = [None, BAR6]
            wdn_dma = [BAR6, BAR6]
            for it in range(64):
                pw = it % 2
                t_c = K.dma("gpsimd", wdnb[pw][:], inp["wdown"][it * 128:(it + 1) * 128, :],
                            waits=[wdn_dma[pw]])
                wdn_dma[pw] = K.dma("sync", S_WDNB[it * 128:(it + 1) * 128, :],
                                    wdnb[pw][:], waits=[t_c])
            og_done = {}
            og_dma = [None, None]
            for t in range(KT):
                p = t % 2
                t_lo = K.dma("sync", otile[p][:], S_OT[t * 128:(t + 1) * 128, :],
                             waits=[og_done.get(t - 2), BAR6] + ([K.t("qS")] if t == 0 else []))
                t_lg = K.dma("sync", gtile[p][:], S_GATE[t * 128:(t + 1) * 128, :],
                             waits=[og_done.get(t - 2)])
                t_sq = K.step("scalar", lambda e, p=p: e.activation(
                    sqt[:], otile[p][:], AF.Square), waits=[t_lo, og_done.get(t - 1)])
                def vms(e):
                    K.edge(e, e.tensor_reduce(mst[:], sqt.ap().rearrange("p (h n) -> p h n", h=H),
                                              AX, ALU.add), "vector")
                    return e.tensor_scalar(ms2[:], mst[:], 1.0 / 128, 1e-6, ALU.mult, ALU.add)
                t_ms = K.step("vector", vms, waits=[t_sq])
                def ars(e):
                    K.edge(e, e.activation(mst[:], ms2[:], AF.Ln))
                    return e.activation(rst[:], mst[:], AF.Exp, scale=-0.5)
                t_rs = K.step("scalar", ars, waits=[t_ms])
                def vog(e, p=p):
                    e.tensor_tensor(t1t.ap().rearrange("p (h n) -> p h n", h=H),
                                    otile[p].ap().rearrange("p (h n) -> p h n", h=H),
                                    rst.ap().unsqueeze(2).broadcast_to([128, H, 128]), ALU.mult)
                    return e.tensor_tensor(ogt[p][:], t1t[:], gtile[p][:], ALU.mult)
                t_og = K.step("vector", vog, waits=[t_rs, t_lg, og_dma[p]])
                og_done[t] = t_og
                og_dma[p] = K.dma("sync", S_OG[t * 128:(t + 1) * 128, :], ogt[p][:], waits=[t_og])
            ph6.close()

        if phase_limit >= 7 and not skip_mid:
            K.new_epoch()
            ph7 = ExitStack()
            e7 = ph7.enter_context
            oT = e7(nc.sbuf_tensor("p7oT", [128, KT * TOK], BF16))
            wob = [e7(nc.sbuf_tensor(f"p7w{i}", [128, KT * 512], BF16)) for i in range(2)]
            xns = [e7(nc.sbuf_tensor(f"p7xn{i}", [128, 512], BF16)) for i in range(2)]
            x1c = [e7(nc.sbuf_tensor(f"p7x1{i}", [128, 512], F32)) for i in range(2)]
            sqs = e7(nc.sbuf_tensor("p7sq", [128, 512], F32))
            sst = e7(nc.sbuf_tensor("p7ss", [128, 64], F32))
            ps8 = [e7(nc.psum_tensor(f"p7p{i}", [128, 512], F32)) for i in range(8)]
            BAR7 = K.barrier()
            t_oT = None
            for vd in range(KT):
                t_oT = K.dma_tr(oT[:, vd * TOK:(vd + 1) * TOK], S_OG[:, vd * 128:(vd + 1) * 128],
                                waits=[K.t("qS"), BAR7] if vd == 0 else ())
            psum_free = None
            ws_done = {}
            x1_done = {}
            x1_spill = {}
            ss_done = {}
            for dchunk in range(4):
                p = dchunk % 2
                t_wl = None
                for qq in range(2):
                    t_wl = K.dma("gpsimd",
                                 wob[p][:, qq * 4096:(qq + 1) * 4096]
                                 .rearrange("p (kt m) -> p kt m", kt=8),
                                 inp["wo"][qq * 1024:(qq + 1) * 1024,
                                           dchunk * 512:(dchunk + 1) * 512]
                                 .rearrange("(kt p) m -> p kt m", p=128),
                                 waits=[x1_done.get((dchunk - 2, 1, 7)), BAR7] if qq == 0 else ())
                t_ws = K.step("vector", lambda e, p=p: e.tensor_scalar(
                    wob[p][:], wob[p][:], onormc[:, 0:1], None, ALU.mult),
                    waits=[t_wl, ws_done.get(dchunk - 2)])
                ws_done[dchunk] = t_ws
                if "WOB" in dbg and dchunk < 2:
                    for blk in range(4):
                        ws_done[dchunk] = K.dma(
                            "sync", S_M[dchunk * 512 + blk * 128:dchunk * 512 + (blk + 1) * 128, :],
                            wob[p][:, blk * 2048:(blk + 1) * 2048], waits=[t_ws])
                for half in range(2):
                    mm_tok = None
                    for vd in range(KT):
                        def tmo(e, vd=vd, half=half, p=p):
                            ins = None
                            for tk in range(8):
                                tt = half * 8 + tk
                                ins = e.matmul(ps8[tk][:, 0:512],
                                               oT[:, vd * TOK + tt * 128:vd * TOK + (tt + 1) * 128],
                                               wob[p][:, vd * 512:(vd + 1) * 512],
                                               start=(vd == 0), stop=(vd == KT - 1))
                            return ins
                        wl = [t_oT, t_ws]
                        if vd == 0:
                            wl.append(psum_free)
                        mm_tok = K.step("tensor", tmo, waits=wl)
                    ps_dump = None
                    if "PS7" in dbg and half == 0:
                        t_cp = K.step("scalar", lambda e: e.activation(
                            sqs[:], ps8[0][:, 0:512], AF.Copy), waits=[mm_tok])
                        ps_dump = K.dma("gpsimd",
                                        S_M[4096 + dchunk * 128:4096 + (dchunk + 1) * 128, 0:512],
                                        sqs[:], waits=[t_cp])
                    for tk in range(8):
                        tt = half * 8 + tk
                        pq = tk % 2
                        t_xn = K.dma("sync", xns[pq][:],
                                     S_XN[tt * 128:(tt + 1) * 128, dchunk * 512:(dchunk + 1) * 512],
                                     waits=[x1_done.get((dchunk, half, tk - 2))])
                        def vx1(e, tk=tk, pq=pq):
                            return e.tensor_add(x1c[pq][:], ps8[tk][:, 0:512], xns[pq][:])
                        t_x1 = K.step("vector", vx1,
                                      waits=[mm_tok, t_xn, ss_done.get((dchunk, half, tk - 2)),
                                             x1_spill.get((dchunk, half, tk - 2))])
                        x1_done[(dchunk, half, tk)] = t_x1
                        x1_spill[(dchunk, half, tk)] = K.dma(
                            "sync", S_X1[tt * 128:(tt + 1) * 128, dchunk * 512:(dchunk + 1) * 512],
                            x1c[pq][:], waits=[t_x1])
                        t_ss = K.step("scalar", lambda e, tt=tt, dchunk=dchunk, pq=pq: e.activation(
                            sqs[:], x1c[pq][:], AF.Square,
                            accum_out=sst[:, tt * 4 + dchunk:tt * 4 + dchunk + 1]),
                            waits=[t_x1, ps_dump if tk == 0 else None])
                        ss_done[(dchunk, half, tk)] = t_ss
                    psum_free = x1_done[(dchunk, half, 7)]
            last_ss = ss_done[(3, 1, 7)]
            if "S_DBG" in dbg:
                K.dma("sync", S_DBG[:], sst[:, 0:64], waits=[last_ss])
            dbg_rflag = "S_DBG" in dbg
            x1t = [e7(nc.sbuf_tensor(f"p7xt{i}", [128, D], F32)) for i in range(2)]
            h2t = [e7(nc.sbuf_tensor(f"p7h2{i}", [128, D], BF16)) for i in range(2)]
            rcol = e7(nc.sbuf_tensor("p7rc", [128, 2], F32))
            h2_done = {}
            h2_dma = [None, None]
            rdbg = e7(nc.sbuf_tensor("p7rdbg", [128, 32], F32))
            dbgsnap = e7(nc.sbuf_tensor("p7snap", [128, 2 * D + 8], F32))
            for t in range(KT):
                p = t % 2
                t_lx = K.dma("sync", x1t[p][:], S_X1[t * 128:(t + 1) * 128, :],
                             waits=[h2_done.get(t - 2)] + ([K.t("qS")] if t == 0 else []))
                def vr(e, t=t):
                    K.edge(e, e.tensor_reduce(rcol[:, 0:1], sst[:, t * 4:(t + 1) * 4],
                                              AX, ALU.add), "vector")
                    return e.tensor_scalar(rcol[:, 0:1], rcol[:, 0:1], 1.0 / D, 1e-6,
                                           ALU.mult, ALU.add)
                t_vr = K.step("vector", vr, waits=[last_ss, h2_done.get(t - 1)])
                def ar(e):
                    K.edge(e, e.activation(rcol[:, 1:2], rcol[:, 0:1], AF.Ln))
                    return e.activation(rcol[:, 0:1], rcol[:, 1:2], AF.Exp, scale=-0.5)
                t_ar = K.step("scalar", ar, waits=[t_vr])
                def vh2(e, p=p, t=t):
                    if t == 0 and "S_DBG2" in dbg:
                        e.tensor_copy(dbgsnap[:, 0:D], x1t[p][:])
                        e.tensor_copy(dbgsnap[:, D:D + 1], rcol[:, 0:1])
                        e.tensor_copy(dbgsnap[:, D + 4:2 * D + 4], ln2bc[:])
                    return e.scalar_tensor_tensor(h2t[p][:], x1t[p][:], rcol[:, 0:1],
                                                  ln2bc[:], ALU.mult, ALU.mult)
                t_h2 = K.step("vector", vh2, waits=[t_ar, t_lx, t_ln2bc, h2_dma[p]])
                h2_done[t] = t_h2
                h2_dma[p] = K.dma("sync", S_H2[t * 128:(t + 1) * 128, :], h2t[p][:], waits=[t_h2])
            if "S_DBG2" in dbg:
                K.dma("sync", S_DBG2.bitcast(F32)[:, 0:1024], dbgsnap[:, 0:1024],
                      waits=[h2_done[KT - 1]])
                K.dma("sync", S_DBG3[:], dbgsnap[:, 1024:1024 + 3 * 1024], waits=[h2_done[KT - 1]])
            ph7.close()

        if phase_limit >= 8 and not skip_mid:
            K.new_epoch()
            ph8 = ExitStack()
            e8 = ph8.enter_context
            h2T = e8(nc.sbuf_tensor("p8h2T", [128, KT * TOK], BF16))
            wgb_ = [e8(nc.sbuf_tensor(f"p8wg{i}", [128, D], BF16)) for i in range(2)]
            wub_ = [e8(nc.sbuf_tensor(f"p8wu{i}", [128, D], BF16)) for i in range(2)]
            sgb = e8(nc.sbuf_tensor("p8sg", [128, 512], BF16))
            mb = [e8(nc.sbuf_tensor(f"p8m{i}", [128, 512], BF16)) for i in range(2)]
            psg = e8(nc.psum_tensor("p8pg", [128, 512], F32))
            psu = e8(nc.psum_tensor("p8pu", [128, 512], F32))
            BAR8 = K.barrier()
            t_h2T = None
            for dt_i in range(KT):
                t_h2T = K.dma_tr(h2T[:, dt_i * TOK:(dt_i + 1) * TOK],
                                 S_H2[:, dt_i * 128:(dt_i + 1) * 128],
                                 waits=[K.t("qS"), BAR8] if dt_i == 0 else ())
            m_dma = [None, None]
            prev_m = None
            prev_it_m = {}
            for it in range(64):
                p = it % 2
                for qq in range(2):
                    K.dma("gpsimd", wgb_[p][:, qq * 1024:(qq + 1) * 1024]
                          .rearrange("p (kt m) -> p kt m", kt=8),
                          inp["wgate"][qq * 1024:(qq + 1) * 1024, it * 128:(it + 1) * 128]
                          .rearrange("(kt p) m -> p kt m", p=128),
                          waits=[prev_it_m.get(it - 2), BAR8] if qq == 0 else ())
                t_wl = None
                for qq in range(2):
                    t_wl = K.dma("gpsimd", wub_[p][:, qq * 1024:(qq + 1) * 1024]
                                 .rearrange("p (kt m) -> p kt m", kt=8),
                                 inp["wup"][qq * 1024:(qq + 1) * 1024, it * 128:(it + 1) * 128]
                                 .rearrange("(kt p) m -> p kt m", p=128))
                for tc in range(4):
                    def tg(e, p=p, tc=tc):
                        ins = None
                        for k in range(KT):
                            ins = e.matmul(psg[:, 0:512], wgb_[p][:, k * 128:(k + 1) * 128],
                                           h2T[:, k * TOK + tc * 512:k * TOK + (tc + 1) * 512],
                                           start=(k == 0), stop=(k == KT - 1))
                        for k in range(KT):
                            ins = e.matmul(psu[:, 0:512], wub_[p][:, k * 128:(k + 1) * 128],
                                           h2T[:, k * TOK + tc * 512:k * TOK + (tc + 1) * 512],
                                           start=(k == 0), stop=(k == KT - 1))
                        return ins
                    t_mm = K.step("tensor", tg, waits=[t_h2T, t_wl, prev_m])
                    t_sg = K.step("scalar", lambda e: e.activation(sgb[:], psg[:, 0:512], AF.Silu),
                                  waits=[t_mm])
                    pq = tc % 2
                    t_m = K.step("vector", lambda e, pq=pq: e.tensor_tensor(
                        mb[pq][:], psu[:, 0:512], sgb[:], ALU.mult),
                        waits=[t_sg, m_dma[pq]])
                    prev_m = t_m
                    m_dma[pq] = K.dma("sync", S_M[it * 128:(it + 1) * 128, tc * 512:(tc + 1) * 512],
                                      mb[pq][:], waits=[t_m])
                prev_it_m[it] = prev_m
            K.new_epoch()
            BAR8D = K.barrier()
            mch = e8(nc.sbuf_tensor("p8mc", [128, 64 * 512], BF16))
            wdn = [e8(nc.sbuf_tensor(f"p8wd{i}", [128, 512], BF16)) for i in range(2)]
            x1s = [e8(nc.sbuf_tensor(f"p8x1{i}", [128, 512], F32)) for i in range(2)]
            sout = [e8(nc.sbuf_tensor(f"p8s{i}", [128, 512], F32)) for i in range(2)]
            ps4 = [e8(nc.psum_tensor(f"p8p4{i}", [128, 512], F32)) for i in range(4)]
            s_dma = [None, None]
            ev_d = None
            t_mch_prev = None
            for tc in range(4):
                t_mch = None
                for qq in range(8):
                    t_mch = K.dma("sync",
                                  mch[:, qq * 4096:(qq + 1) * 4096]
                                  .rearrange("p (kt m) -> p kt m", kt=8),
                                  S_M[qq * 1024:(qq + 1) * 1024, tc * 512:(tc + 1) * 512]
                                  .rearrange("(kt p) m -> p kt m", p=128),
                                  waits=[m_dma[0], m_dma[1], ev_d, BAR8D] if qq == 0 else ())
                for dchunk in range(4):
                    wl_tok = None
                    mm_tok = None
                    for it in range(64):
                        p = it % 2
                        wl_tok = K.dma("sync", wdn[p][:],
                                       S_WDNB[it * 128:(it + 1) * 128,
                                              dchunk * 512:(dchunk + 1) * 512],
                                       waits=[mm_tok])
                        def td(e, it=it, p=p):
                            ins = None
                            for sub in range(4):
                                ins = e.matmul(ps4[sub][:, 0:512],
                                               mch[:, it * 512 + sub * 128:it * 512 + (sub + 1) * 128],
                                               wdn[p][:], start=(it == 0), stop=(it == 63))
                            return ins
                        mm_tok = K.step("tensor", td,
                                        waits=[t_mch, wl_tok] + ([ev_d] if it == 0 else []))
                    for sub in range(4):
                        tt = tc * 4 + sub
                        pq = sub % 2
                        t_x1 = K.dma("sync", x1s[pq][:],
                                     S_X1[tt * 128:(tt + 1) * 128, dchunk * 512:(dchunk + 1) * 512],
                                     waits=[s_dma[pq]])
                        def vs(e, sub=sub, pq=pq):
                            return e.tensor_add(sout[pq][:], ps4[sub][:, 0:512], x1s[pq][:])
                        t_s = K.step("vector", vs, waits=[mm_tok, t_x1, s_dma[pq]])
                        ev_d = t_s
                        s_dma[pq] = K.dma("sync",
                                          S_S[tt * 128:(tt + 1) * 128, dchunk * 512:(dchunk + 1) * 512],
                                          sout[pq][:], waits=[t_s])
            ph8.close()

        if phase_limit >= 9:
            K.new_epoch()
            ph9 = ExitStack()
            e9_outer = ph9.enter_context
            ph9a = ExitStack()
            e9 = ph9a.enter_context
            pkb = [e9(nc.sbuf_tensor(f"p9pk{i}", [128, 1408 // 2], I32)) for i in range(2)]
            scf = e9(nc.sbuf_tensor("p9scf", [44, 128], F16))
            scf32 = e9(nc.sbuf_tensor("p9scf32", [44, 128], F32))
            scT = e9(nc.sbuf_tensor("p9scT", [128, 44], F32))
            scE = e9(nc.sbuf_tensor("p9scE", [128, 1408], BF16))
            s16a = e9(nc.sbuf_tensor("p9s16a", [128, 1408], I16))
            s16b = e9(nc.sbuf_tensor("p9s16b", [128, 1408], I16))
            s16c = e9(nc.sbuf_tensor("p9s16c", [128, 1408], I16))
            s16d = e9(nc.sbuf_tensor("p9s16d", [128, 1408], I16))
            s16e = e9(nc.sbuf_tensor("p9s16e", [128, 1408], I16))
            g2b = e9(nc.sbuf_tensor("p9g2", [128, 1408], BF16))
            mnb = e9(nc.sbuf_tensor("p9mn", [128, 1408], BF16))
            tb_ = e9(nc.sbuf_tensor("p9tb", [128, 1408], BF16))
            dq = [e9(nc.sbuf_tensor(f"p9dq{i}", [128, I_F], BF16)) for i in range(2)]
            dqst = e9(nc.sbuf_tensor("p9dqst", [128, I_F], BF16))
            ps_sc = e9(nc.psum_tensor("p9psc", [128, 512], F32))

            BAR9 = K.barrier()
            def dq_tile(pk_src, sc_src, n32, ngr, dq_rows, dq_dst, prev_toks):
                p = dq_tile.p
                dq_tile.p = 1 - p
                n16 = n32 * 2
                t_pk = K.dma("gpsimd", pkb[p][:, 0:n32], pk_src,
                             waits=[prev_toks.get("dq", None), BAR9])
                t_sc = K.dma("sync", scf[0:ngr, :], sc_src, waits=[prev_toks.get("sc", None)])
                t_sc2 = K.step("vector", lambda e: e.tensor_copy(scf32[0:ngr, :], scf[0:ngr, :]),
                               waits=[t_sc, prev_toks.get("scT", None)])
                t_tr = K.step("tensor", lambda e: e.transpose(
                    ps_sc[:, 0:ngr], scf32[0:ngr, :], identf[0:ngr, 0:ngr]),
                    waits=[t_sc2, prev_toks.get("scE", None)])
                t_scT = K.step("vector", lambda e: e.tensor_copy(scT[:, 0:ngr], ps_sc[:, 0:ngr]),
                               waits=[t_tr])
                prev_toks["scT"] = t_scT
                t_scE = K.step("vector", lambda e: e.tensor_scalar(
                    scE[:, 0:32 * ngr].rearrange("p (g r) -> p g r", r=32),
                    scT[:, 0:ngr].unsqueeze(2).broadcast_to([128, ngr, 32]),
                    4.0, None, ALU.mult), waits=[t_scT])
                prev_toks["scE"] = t_scE
                pk16 = pkb[p].ap().bitcast(I16)[:, 0:n16]
                t_last = None
                snap = "S_DBG3" in dbg and dq_tile.snap
                for j2 in range(4):
                    def vdec(e, j2=j2, pk16=pk16, n16=n16):
                        e.tensor_single_scalar(s16a[:, 0:n16], pk16, 4 * j2,
                                               ALU.logical_shift_right)
                        e.tensor_single_scalar(s16b[:, 0:n16], s16a[:, 0:n16], 7, ALU.bitwise_and)
                        e.tensor_single_scalar(s16c[:, 0:n16], s16a[:, 0:n16], 8, ALU.bitwise_and)
                        e.tensor_scalar(s16d[:, 0:n16], s16b[:, 0:n16], 6, 0x3E00,
                                        ALU.logical_shift_left, ALU.bitwise_or)
                        return e.tensor_scalar(s16e[:, 0:n16], s16c[:, 0:n16], 12, 0x3F80,
                                               ALU.logical_shift_left, ALU.bitwise_or)
                    t_v = K.step("vector", vdec, waits=[t_pk, t_last])
                    t_g2 = K.step("scalar", lambda e, n16=n16: e.activation(
                        g2b[:, 0:n16], s16d.ap().bitcast(BF16)[:, 0:n16], AF.Copy,
                        bias=-0.25, scale=2.0), waits=[t_v])
                    def vfin(e, j2=j2, n16=n16):
                        e.tensor_tensor(mnb[:, 0:n16], g2b[:, 0:n16],
                                        s16d.ap().bitcast(BF16)[:, 0:n16], ALU.min)
                        e.tensor_tensor(tb_[:, 0:n16], mnb[:, 0:n16],
                                        s16e.ap().bitcast(BF16)[:, 0:n16], ALU.mult)
                        return e.tensor_tensor(dqst[:, j2 * n16:(j2 + 1) * n16],
                                               tb_[:, 0:n16], scE[:, 0:n16], ALU.mult)
                    t_last = K.step("vector", vfin, waits=[t_g2])
                if snap:
                    def vsnap(e, n16=n16):
                        e.tensor_copy(dbq1[:, 0:512], s16d.ap().bitcast(BF16)[:, 0:512])
                        e.tensor_copy(dbq1[:, 512:1024], s16e.ap().bitcast(BF16)[:, 0:512])
                        e.tensor_copy(dbq1[:, 1024:1536], g2b[:, 0:512])
                        e.tensor_copy(dbq1[:, 1536:2048], mnb[:, 0:512])
                        e.tensor_copy(dbq1[:, 2048:2560], tb_[:, 0:512])
                        return e.tensor_copy(dbq1[:, 2560:3072], scE[:, 0:512])
                    t_last = K.step("vector", vsnap, waits=[t_last])
                    dq_tile.snap = False
                def vint(e, n16=n16):
                    return e.tensor_copy(
                        dq_dst.rearrange("p (m four) -> p m four", four=4),
                        dqst.ap()[:, 0:4 * n16].rearrange("p (four m) -> p m four", four=4))
                t_last = K.step("vector", vint, waits=[t_last, prev_toks.get("st", None)])
                t_st = K.dma("sync", dq_rows, dq_dst, waits=[t_last])
                prev_toks["dq"] = t_last
                prev_toks["st"] = t_st
                return t_st
            dq_tile.p = 0
            dq_tile.snap = True
            dbq1 = e9(nc.sbuf_tensor("p9dbq", [128, 3072], BF16))
            if "S_DBG3" in dbg:
                pass

            ptoks = {}
            t_dq_last = None
            for ot in range(88 if "deq" in p9_parts else 0):
                t_dq_last = dq_tile(
                    inp["gup"][ot * 128:(ot + 1) * 128, :],
                    inp["gus"][:, ot * 128:(ot + 1) * 128],
                    256, 16,
                    S_WGU[ot * 128:(ot + 1) * 128, :],
                    dq[ot % 2][:, 0:2048], ptoks)
            for ot in range(16 if "deq" in p9_parts else 0):
                t_dq_last = dq_tile(
                    inp["dnp"][ot * 128:(ot + 1) * 128, :],
                    inp["dns"][:, ot * 128:(ot + 1) * 128],
                    704, 44,
                    S_WDN4[ot * 128:(ot + 1) * 128, :],
                    dq[ot % 2][:, 0:I_F], ptoks)
            if "S_DBG3" in dbg:
                K.dma("sync", S_DBG3.bitcast(BF16)[:, 0:3072], dbq1[:, 0:3072], waits=[t_dq_last])
            t_dq_done = K.t("qS")
            K.new_epoch()
            ph9a.close()

            ph9b = ExitStack()
            e9 = ph9b.enter_context
            xnT = e9(nc.sbuf_tensor("p9xnT", [128, KT * TOK], BF16))
            t_xnT = None
            for dt_i in range(KT):
                t_xnT = K.dma_tr(xnT[:, dt_i * TOK:(dt_i + 1) * TOK],
                                 S_XN[:, dt_i * 128:(dt_i + 1) * 128],
                                 waits=[K.t("qS")] if dt_i == 0 else ())
            wtg = [e9(nc.sbuf_tensor(f"p9wg{i}", [128, KT * 512], BF16)) for i in range(2)]
            wtu = [e9(nc.sbuf_tensor(f"p9wu{i}", [128, KT * 512], BF16)) for i in range(2)]
            sg4 = e9(nc.sbuf_tensor("p9sg", [128, 512], BF16))
            h4b_ = [e9(nc.sbuf_tensor(f"p9h4{i}", [128, 512], BF16)) for i in range(2)]
            psg4 = e9(nc.psum_tensor("p9pg", [128, 512], F32))
            psu4 = e9(nc.psum_tensor("p9pu", [128, 512], F32))
            h4_dma = [None, None]
            prev_h4 = None
            prev_grp = {}
            BAR9B = K.barrier()
            for op_i in range(11 if "gu" in p9_parts else 0):
                p = op_i % 2
                t_wt = None
                for kin in range(KT):
                    K.dma_tr(wtg[p][:, kin * 512:(kin + 1) * 512],
                             S_WGU[op_i * 512:(op_i + 1) * 512, kin * 128:(kin + 1) * 128],
                             waits=[t_dq_done, prev_grp.get(op_i - 2), BAR9B] if kin == 0 else ())
                for kin in range(KT):
                    t_wt = K.dma_tr(wtu[p][:, kin * 512:(kin + 1) * 512],
                                    S_WGU[I_F + op_i * 512:I_F + (op_i + 1) * 512,
                                          kin * 128:(kin + 1) * 128])
                for ot4 in range(4):
                    for tc in range(4):
                        def tg4(e, p=p, ot4=ot4, tc=tc):
                            ins = None
                            for k in range(KT):
                                ins = e.matmul(psg4[:, 0:512],
                                               wtg[p][:, k * 512 + ot4 * 128:k * 512 + (ot4 + 1) * 128],
                                               xnT[:, k * TOK + tc * 512:k * TOK + (tc + 1) * 512],
                                               start=(k == 0), stop=(k == KT - 1))
                            for k in range(KT):
                                ins = e.matmul(psu4[:, 0:512],
                                               wtu[p][:, k * 512 + ot4 * 128:k * 512 + (ot4 + 1) * 128],
                                               xnT[:, k * TOK + tc * 512:k * TOK + (tc + 1) * 512],
                                               start=(k == 0), stop=(k == KT - 1))
                            return ins
                        t_mm = K.step("tensor", tg4, waits=[t_xnT, t_wt, prev_h4])
                        t_sg4 = K.step("scalar", lambda e: e.activation(
                            sg4[:], psg4[:, 0:512], AF.Silu), waits=[t_mm])
                        pq = tc % 2
                        t_h4 = K.step("vector", lambda e, pq=pq: e.tensor_tensor(
                            h4b_[pq][:], psu4[:, 0:512], sg4[:], ALU.mult),
                            waits=[t_sg4, h4_dma[pq]])
                        prev_h4 = t_h4
                        rowi = op_i * 4 + ot4
                        h4_dma[pq] = K.dma("sync",
                                           S_H4[rowi * 128:(rowi + 1) * 128, tc * 512:(tc + 1) * 512],
                                           h4b_[pq][:], waits=[t_h4])
                prev_grp[op_i] = prev_h4
            t_h4_done = K.t("qS")
            K.new_epoch()
            ph9b.close()

            ph9c = ExitStack()
            e9 = ph9c.enter_context
            h4c = e9(nc.sbuf_tensor("p9h4c", [128, 44 * 512], BF16))
            wdt = [e9(nc.sbuf_tensor(f"p9wd{i}", [128, 512], BF16)) for i in range(2)]
            ypark = e9(nc.sbuf_tensor("p9yp", [128, 4 * D], F32))
            yss = e9(nc.sbuf_tensor("p9yss", [128, 16], F32))
            ysq = e9(nc.sbuf_tensor("p9ysq", [128, 512], F32))
            rstc = e9(nc.sbuf_tensor("p9rst", [128, 2], F32))
            gbb = e9(nc.sbuf_tensor("p9gb", [128, D], BF16))
            xfin = [e9(nc.sbuf_tensor(f"p9xf{i}", [128, D], F32)) for i in range(2)]
            sfin = [e9(nc.sbuf_tensor(f"p9sf{i}", [128, D], F32)) for i in range(2)]
            tfin = e9(nc.sbuf_tensor("p9tf", [128, D], F32))
            obuf = [e9(nc.sbuf_tensor(f"p9ob{i}", [128, D], F32)) for i in range(2)]
            ps44 = [e9(nc.psum_tensor(f"p9p4{i}", [128, 512], F32)) for i in range(4)]
            out_dma = [None, None]
            prev_park = None
            prev_gb = None
            dn_mm_prev = None
            mm_tok = None
            BAR9C = K.barrier()
            ntq = 4 if ("down" in p9_parts or "downld" in p9_parts) else 0
            if "tq1" in p9_parts:
                ntq = 1
            for tq in range(ntq):
                t_h4c = None
                for i4 in range(44):
                    t_h4c = K.dma("sync", h4c[:, i4 * 512:(i4 + 1) * 512],
                                  S_H4[i4 * 128:(i4 + 1) * 128, tq * 512:(tq + 1) * 512],
                                  waits=[t_h4_done, prev_park, BAR9C] if i4 == 0 else ())
                park_toks = {}
                ndc = 4 if ("downmm" in p9_parts or "down" in p9_parts) else 0
                if "dc1" in p9_parts:
                    ndc = 1
                for dchunk in range(ndc):
                    wl_tok = None
                    psum_guard = park_toks.get((dchunk - 1, 3), prev_park)
                    for i4 in range(44):
                        pw = i4 % 2
                        wl_tok = K.dma_tr(wdt[pw][:],
                                          S_WDN4[dchunk * 512:(dchunk + 1) * 512,
                                                 i4 * 128:(i4 + 1) * 128],
                                          waits=[mm_tok])
                        def td4(e, i4=i4, pw=pw):
                            ins = None
                            for sub in range(4):
                                ins = e.matmul(ps44[sub][:, 0:512],
                                               h4c[:, i4 * 512 + sub * 128:i4 * 512 + (sub + 1) * 128],
                                               wdt[pw][:], start=(i4 == 0), stop=(i4 == 43))
                            return ins
                        mm_tok = K.step("tensor", td4,
                                        waits=[t_h4c, wl_tok] + ([psum_guard] if i4 == 0 else []))
                    for sub in range(4 if ("downpark" in p9_parts or "down" in p9_parts) else 0):
                        def vpark(e, sub=sub, dchunk=dchunk):
                            return e.tensor_copy(
                                ypark[:, sub * D + dchunk * 512:sub * D + (dchunk + 1) * 512],
                                ps44[sub][:, 0:512])
                        t_pk2 = K.step("vector", vpark,
                                       waits=[mm_tok, prev_gb if dchunk == 0 else None])
                        if "downsq" in p9_parts or "down" in p9_parts:
                            t_ysq = K.step("scalar", lambda e, sub=sub, dchunk=dchunk: e.activation(
                                ysq[:], ypark[:, sub * D + dchunk * 512:sub * D + (dchunk + 1) * 512],
                                AF.Square, accum_out=yss[:, sub * 4 + dchunk:sub * 4 + dchunk + 1]),
                                waits=[t_pk2])
                            park_toks[(dchunk, sub)] = t_ysq
                        else:
                            park_toks[(dchunk, sub)] = t_pk2
                prev_park = park_toks.get((3, 3), mm_tok if ndc else t_h4c)
                for sub in range(4 if "fin" in p9_parts or "down" in p9_parts and "nofin" not in p9_parts else 0):
                    tt = tq * 4 + sub
                    p = sub % 2
                    t_xl = K.dma("sync", xfin[p][:], inp["x"][tt * 128:(tt + 1) * 128, :],
                                 waits=[out_dma[p]])
                    t_sl = K.dma("sync", sfin[p][:], S_S[tt * 128:(tt + 1) * 128, :])
                    def vrst(e, sub=sub):
                        K.edge(e, e.tensor_reduce(rstc[:, 0:1], yss[:, sub * 4:(sub + 1) * 4],
                                                  AX, ALU.add), "vector")
                        return e.tensor_scalar(rstc[:, 0:1], rstc[:, 0:1], 1.0 / D, 1e-6,
                                               ALU.mult, ALU.add)
                    t_vr = K.step("vector", vrst, waits=[prev_park])
                    def arst(e):
                        K.edge(e, e.activation(rstc[:, 1:2], rstc[:, 0:1], AF.Ln))
                        return e.activation(rstc[:, 0:1], rstc[:, 1:2], AF.Exp, scale=-0.5)
                    t_ar = K.step("scalar", arst, waits=[t_vr])
                    t_gb = K.step("scalar", lambda e, sub=sub: e.activation(
                        gbb[:], ypark[:, sub * D:(sub + 1) * D], AF.Silu, scale=rstc[:, 0:1]),
                        waits=[t_ar])
                    prev_gb = t_gb
                    def vfin2(e, p=p):
                        e.tensor_tensor(tfin[:], gbb[:], sfin[p][:], ALU.mult)
                        return e.tensor_add(obuf[p][:], tfin[:], xfin[p][:])
                    t_of = K.step("vector", vfin2, waits=[t_gb, t_xl, t_sl, out_dma[p]])
                    out_dma[p] = K.dma("sync", out[tt * 128:(tt + 1) * 128, :], obuf[p][:],
                                       waits=[t_of])
            ph9c.close()
            ph9.close()

        if "S_DBG2" in dbg:
            def lndump(e):
                for s in ["sT", "sV", "sA", "sG"]:
                    if cnt[s]:
                        e.wait_ge(sem[s], cnt[s])
                q = K.cur("qS")
                cnt[q] += 16
                e.dma_start(out=S_DBG2[:], in_=ln2bc[:]).then_inc(sem[q], 16)
            K.block.sync(lndump)
        def fin(e):
            for s, v in cnt.items():
                if v and s != "sS":
                    e.wait_ge(sem[s], v)
            return e.nop()
        K.step("sync", fin)

    build_nc.last_K = K
    return nc, inp, out


def shard_inputs(inputs):
    consts = host_constants()
    x = inputs["L0_attn_x"]
    p = lambda k: np.asarray(inputs["L0_attn_" + k])
    conv = np.stack([p("conv_q"), p("conv_k"), p("conv_v")])
    convw = np.ascontiguousarray(
        conv.reshape(3, 4, KT, 128).transpose(3, 0, 2, 1).reshape(128, 3 * KT * 4))
    common = {
        "wq": p("wq"), "wk": p("wk"), "wv": p("wv"), "convw": convw,
        "wfa": p("wfa"), "wfb": p("wfb"), "wb": p("wb"), "wga": p("wga"), "wgb": p("wgb"),
        "alog": p("a_log").reshape(H, 1), "dtb": p("dt_bias").reshape(H, 1),
        "onorm": p("o_norm_w").reshape(128, 1), "wo": p("wo"),
        "ln1w": np.ascontiguousarray(p("ln1_w").reshape(KT, 128).T),
        "ln2w": p("ln2_w").reshape(1, D),
        "wgate": p("w_gate"), "wup": p("w_up"), "wdown": p("w_down"),
        "gup": np.asarray(inputs["L0_ffn_gate_up_packed"]),
        "gus": np.asarray(inputs["L0_ffn_gate_up_scales"]),
        "dnp": np.asarray(inputs["L0_ffn_down_packed"]),
        "dns": np.asarray(inputs["L0_ffn_down_scales"]),
    }
    common.update(consts)
    maps = []
    for c in range(8):
        b, s = c // 4, c % 4
        m = dict(common)
        m["x"] = np.ascontiguousarray(np.asarray(x[b, s * TOK:(s + 1) * TOK], np.float32))
        if s == 0:
            m["xh"] = np.zeros((3, D), np.float32)
        else:
            m["xh"] = np.ascontiguousarray(np.asarray(x[b, s * TOK - 3:s * TOK], np.float32))
        maps.append(m)
    return maps


def kernel(**inputs):
    maps = shard_inputs(inputs)
    nc, _, _ = build_nc(dbg=DBG, phase_limit=PHASE_LIMIT)
    res = run_bass_kernel_spmd(nc, maps, core_ids=list(range(8)))
    x = inputs["L0_attn_x"]
    outp = np.empty((2, 8192, D), dtype=np.float32)
    for c in range(8):
        b, s = c // 4, c % 4
        outp[b, s * TOK:(s + 1) * TOK] = res.results[c]["out"]
    return outp
